# version 03 (final)
# baseline (speedup 1.0000x reference)
import numpy as np
import contextlib
import concourse.bass as bass
import concourse.mybir as mybir
from concourse.bass_utils import run_bass_kernel_spmd

F32 = mybir.dt.float32
BF16 = mybir.dt.bfloat16
I32 = mybir.dt.int32
AF = mybir.ActivationFunctionType
OP = mybir.AluOpType
AX = mybir.AxisListType.X

T = 4096; DM = 1024; DI = 2048; DH = 1024; NS = 16; DTR = 64
DHID = 4096; TO = 2048; EPS = 1e-5
TC = 512; NCH = T // TC
S3C = 410; NS3 = 5
TCF = 256; NFF = TO // TCF
TCB = 512; NFB = TO // TCB
OUTW = T + 2
K1 = DM * OUTW


class Sem:
    def __init__(self, handle):
        self.h = handle
        self.v = 0

    def inc(self, instr, n=1):
        instr.then_inc(self.h, n)
        self.v += n
        return self.v


def build_nc():
    nc = bass.Bass("TRN2", target_bir_lowering=False)
    st = contextlib.ExitStack()
    E = st.enter_context

    x_fm = nc.dram_tensor("x_fm", [DM, T], F32, kind="ExternalInput").ap()
    x_own = nc.dram_tensor("x_own", [DM, TO + 2], F32, kind="ExternalInput").ap()
    w_in = nc.dram_tensor("w_in", [DM, DI], F32, kind="ExternalInput").ap()
    xp_sl = nc.dram_tensor("xp_sl", [DH, 128], F32, kind="ExternalInput").ap()
    dtw_sl = nc.dram_tensor("dtw_sl", [DTR, DH], F32, kind="ExternalInput").ap()
    dtb_sl = nc.dram_tensor("dtb_sl", [128, 8], F32, kind="ExternalInput").ap()
    alog_sl = nc.dram_tensor("alog_sl", [128, 128], F32, kind="ExternalInput").ap()
    cw_sl = nc.dram_tensor("cw_sl", [128, 32], F32, kind="ExternalInput").ap()
    cb_sl = nc.dram_tensor("cb_sl", [128, 8], F32, kind="ExternalInput").ap()
    d_sl = nc.dram_tensor("d_sl", [128, 8], F32, kind="ExternalInput").ap()
    nw_al = nc.dram_tensor("nw_al", [128, 8], F32, kind="ExternalInput").ap()
    wout_sl = nc.dram_tensor("wout_sl", [DH, DM], F32, kind="ExternalInput").ap()
    w1t = nc.dram_tensor("w1t", [DM, DHID], F32, kind="ExternalInput").ap()
    w2t = nc.dram_tensor("w2t", [DHID, DM], F32, kind="ExternalInput").ap()
    fcw_in = nc.dram_tensor("fcw_in", [128, 96], F32, kind="ExternalInput").ap()
    fcb_in = nc.dram_tensor("fcb_in", [128, 32], F32, kind="ExternalInput").ap()
    iden_in = nc.dram_tensor("iden_in", [128, 128], F32, kind="ExternalInput").ap()
    mask_in = nc.dram_tensor("mask_in", [128, 2], F32, kind="ExternalInput").ap()
    meta = nc.dram_tensor("meta", [1, 8], I32, kind="ExternalInput").ap()
    widx_in = nc.dram_tensor("widx_in", [128, 10], I32, kind="ExternalInput").ap()
    flagrow_in = nc.dram_tensor("flagrow_in", [128, 32], I32, kind="ExternalInput").ap()
    ridx_in = nc.dram_tensor("ridx_in", [128, 80], I32, kind="ExternalInput").ap()
    consts = nc.dram_tensor("consts", [128, 9], F32, kind="ExternalInput").ap()
    out_fm = nc.dram_tensor("out_fm", [DM, TO], F32, kind="ExternalOutput").ap()

    sh_dbc = nc.dram_tensor("sh_dbc", [2, 128, T], F32, kind="Internal", addr_space="Shared")
    sh_out = nc.dram_tensor("sh_out", [2, DM, OUTW], F32, kind="Internal", addr_space="Shared")
    sh_flag = nc.dram_tensor("sh_flag", [256, 32], I32, kind="Internal", addr_space="Shared")
    u_sp = nc.dram_tensor("u_sp", [DH, T], F32, kind="Internal")
    sr_sp = nc.dram_tensor("sr_sp", [DH, T], F32, kind="Internal")
    dbc_sp = nc.dram_tensor("dbc_sp", [128, T], F32, kind="Internal")
    x2_sp = nc.dram_tensor("x2_sp", [DM, TO + 2], F32, kind="Internal")
    hq_sp = nc.dram_tensor("hq_sp", [DHID, TO], BF16, kind="Internal")
    sdq2_sp = nc.dram_tensor("sdq2_sp", [1, TO], F32, kind="Internal")
    w1q_sp = nc.dram_tensor("w1q_sp", [DM, DHID], BF16, kind="Internal")
    w2q_sp = nc.dram_tensor("w2q_sp", [DHID, DM], BF16, kind="Internal")
    bounce = nc.dram_tensor("bounce", [1, 2048], F32, kind="Internal")

    ones1 = E(nc.sbuf_tensor("ones1", [1, 128], F32))
    onescol = E(nc.sbuf_tensor("onescol", [128, 1], F32))
    zcol = E(nc.sbuf_tensor("zcol", [128, 1], F32))
    iden = E(nc.sbuf_tensor("iden", [128, 128], F32))
    maskc = E(nc.sbuf_tensor("maskc", [128, 2], F32))
    At = E(nc.sbuf_tensor("At", [128, 128], F32))
    dtb = E(nc.sbuf_tensor("dtb", [128, 8], F32))
    cwm = E(nc.sbuf_tensor("cwm", [128, 32], F32))
    cbm = E(nc.sbuf_tensor("cbm", [128, 8], F32))
    Dv = E(nc.sbuf_tensor("Dv", [128, 8], F32))
    nw = E(nc.sbuf_tensor("nw", [128, 8], F32))
    fcw = E(nc.sbuf_tensor("fcw", [128, 96], F32))
    fcb = E(nc.sbuf_tensor("fcb", [128, 32], F32))
    smeta = E(nc.sbuf_tensor("smeta", [1, 8], I32))
    swidx = E(nc.sbuf_tensor("swidx", [128, 10], I32))
    sridx = E(nc.sbuf_tensor("sridx", [128, 80], I32))
    cst = E(nc.sbuf_tensor("cst", [128, 9], F32))
    _cvals = [EPS, 1.0 / DM, 1.0, -0.5, -1.0, 127.0, 1e-5, 0.0,
              float(np.log(127.0))]
    for _i, _cv in enumerate(_cvals):
        nc.const_aps.register(F32, _cv, cst[:, _i:_i + 1])
    pollb = E(nc.sbuf_tensor("pollb", [128, 16], I32))
    sflagrow = E(nc.sbuf_tensor("sflagrow", [128, 32], I32))
    mcol1 = E(nc.sbuf_tensor("mcol1", [128, 1], F32))
    mcol2 = E(nc.sbuf_tensor("mcol2", [128, 1], F32))
    dtb16 = E(nc.sbuf_tensor("dtb16", [DTR, T], BF16))
    carry = E(nc.sbuf_tensor("carry", [128, 128], F32))

    s_ld = Sem(E(nc.semaphore("s_ld")))
    s_gs = Sem(E(nc.semaphore("s_gs")))
    s_pe = Sem(E(nc.semaphore("s_pe")))
    s_sc = Sem(E(nc.semaphore("s_sc")))
    s_ve = Sem(E(nc.semaphore("s_ve")))
    s_gg = Sem(E(nc.semaphore("s_gg")))
    s_sp = E(nc.semaphore("s_sp"))

    blk = E(nc.Block())
    REG = {}

    def g_setup(g):
        for nm in ("rj", "rnj", "rT", "rV", "rF", "rC", "rO"):
            REG[nm] = g.alloc_register(nm)
        rj = REG["rj"]
        g.reg_load(rj, smeta[0:1, 0:1])
        g.reg_alu(REG["rnj"], 1, rj, OP.subtract)
        g.reg_mov(REG["rT"], 0)

    def g_off(g, base_reg, mul, const):
        g.reg_mul(REG["rO"], REG[base_reg], mul)
        if const:
            g.reg_add(REG["rO"], REG["rO"], const)
        return REG["rO"]

    def g_flag(g, phase):
        ins = g.indirect_dma_start(
            bass.AP(sh_flag, 0, [[32, 256], [1, 16]]),
            bass.IndirectOffsetOnAxis(swidx[:, 8:9], 0),
            sflagrow[:, 16 * phase:16 * phase + 16], None,
            element_offset=16 * phase)
        s_gs.inc(ins, 16)
        g.wait_ge(s_gs.h, s_gs.v)

    def g_poll(g, phase):
        g.reg_load(REG["rV"], smeta[0:1, 1 + phase:2 + phase])
        g.reg_mov(REG["rC"], 1)
        with g.While(REG["rC"]):
            g.indirect_dma_start(
                pollb[:, :], None,
                bass.AP(sh_flag, 0, [[32, 256], [1, 16]]),
                bass.IndirectOffsetOnAxis(swidx[:, 9:10], 0),
                element_offset=16 * phase).then_inc(s_sp, 16)
            g.reg_add(REG["rT"], REG["rT"], 16)
            g.wait_ge(s_sp, REG["rT"])
            g.reg_load(REG["rF"], pollb[0:1, 0:1])
            g.reg_alu(REG["rC"], REG["rF"], REG["rV"], OP.not_equal)

    with contextlib.ExitStack() as ph:
        Eh = ph.enter_context
        wstage = Eh(nc.sbuf_tensor("wstage", [128, DHID], F32))
        wint = Eh(nc.sbuf_tensor("wint", [128, DHID], I32))
        wq16 = Eh(nc.sbuf_tensor("wq16", [128, DHID], BF16))
        acc1 = Eh(nc.sbuf_tensor("acc1", [128, 1], F32))
        red1 = Eh(nc.sbuf_tensor("red1", [128, 1], F32))
        msc = Eh(nc.sbuf_tensor("msc", [1, 4], F32))
        zrow = Eh(nc.sbuf_tensor("zrow", [128, 1], F32))
        p_s = Eh(nc.psum_tensor("p_s", [1, 4], F32))
        p_c = Eh(nc.psum_tensor("p_c", [128, 1], F32))

        def p0_sync(sy):
            for dst, src in ((smeta[:, :], meta), (cst[:, :], consts),
                             (swidx[:, :], widx_in), (sridx[:, :], ridx_in),
                             (sflagrow[:, :], flagrow_in),
                             (dtb[:, :], dtb_sl),
                             (cwm[:, :], cw_sl), (cbm[:, :], cb_sl),
                             (Dv[:, :], d_sl), (nw[:, :], nw_al),
                             (At[:, :], alog_sl), (fcw[:, :], fcw_in),
                             (fcb[:, :], fcb_in), (iden[:, :], iden_in),
                             (maskc[:, :], mask_in)):
                s_ld.inc(sy.dma_start(out=dst, in_=src), 16)
        blk.sync(p0_sync)
        ld0 = s_ld.v

        def p0_s(s):
            s.wait_ge(s_ld.h, ld0)
            ins = s.activation(At[:, :], At[:, :], AF.Exp)
            s_sc.inc(ins, 1)
        blk.scalar(p0_s)
        scA = s_sc.v

        def p0_v(v):
            v.wait_ge(s_ld.h, ld0)
            v.memset(ones1[:, :], 1.0)
            v.memset(onescol[:, :], 1.0)
            v.memset(zcol[:, :], 0.0)
            v.memset(zrow[:, :], 0.0)
            v.wait_ge(s_sc.h, scA)
            ins = v.tensor_scalar(At[:, :], At[:, :], -1.0, None, OP.mult)
            s_ve.inc(ins, 1)
        blk.vector(p0_v)
        veA = s_ve.v

        def p0_g(g):
            g_setup(g)
            g.wait_ge(s_ve.h, veA)
            for sl in range(2):
                for col in (0, OUTW - 1):
                    for r0 in range(0, DM, 128):
                        ins = g.dma_start(
                            out=bass.AP(sh_out, sl * K1 + r0 * OUTW + col,
                                        [[OUTW, 128], [1, 1]]),
                            in_=zrow[0:128, 0:1],
                            allow_slow_non_contiguous=True)
                        s_gs.inc(ins, 16)
        blk.gpsimd(p0_g)

        for wsrc, wdst, nk, ncols, mcol in (
            (w1t, w1q_sp, 8, DHID, mcol1),
            (w2t, w2q_sp, 32, DM, mcol2),
        ):
            b_ld = s_ld.v
            b_ve = s_ve.v
            def q1_sync(sy, wsrc=wsrc, nk=nk, ncols=ncols, b_ve=b_ve):
                for kt in range(nk):
                    if kt > 0:
                        sy.wait_ge(s_ve.h, b_ve + 2 * kt - 1)
                    else:
                        sy.wait_ge(s_ve.h, b_ve)
                    s_ld.inc(sy.dma_start(out=wstage[:, 0:ncols],
                                          in_=wsrc[kt * 128:(kt + 1) * 128, :]), 16)
            blk.sync(q1_sync)

            def q1_v(v, nk=nk, ncols=ncols, b_ld=b_ld, b_ve=b_ve):
                v.memset(acc1[:, :], 0.0)
                for kt in range(nk):
                    v.wait_ge(s_ld.h, b_ld + 16 * (kt + 1))
                    ins = v.tensor_reduce(red1[:, :], wstage[:, 0:ncols], axis=AX,
                                          op=OP.add, apply_absolute_value=True)
                    s_ve.inc(ins, 1)
                    v.wait_ge(s_ve.h, b_ve + 2 * kt + 1)
                    ins = v.tensor_add(acc1[:, :], acc1[:, :], red1[:, :])
                    s_ve.inc(ins, 1)
            blk.vector(q1_v)
            b_ve2 = s_ve.v

            def q1_t(t, b_ve2=b_ve2):
                t.wait_ge(s_ve.h, b_ve2)
                ins = t.matmul(p_s[0:1, 0:1], acc1[:, :], onescol[:, :],
                               start=True, stop=True)
                s_pe.inc(ins, 1)
            blk.tensor(q1_t)
            b_pe = s_pe.v

            def q1_v2(v, nelem=nk * 128 * ncols, b_pe=b_pe):
                v.wait_ge(s_pe.h, b_pe)
                v.tensor_scalar(msc[0:1, 0:1], p_s[0:1, 0:1], 1.0 / nelem, 1e-5,
                                OP.mult, OP.max)
                ins = v.tensor_scalar(msc[0:1, 2:3], p_s[0:1, 0:1],
                                      1.0 / (nelem * 127.0), 1e-5 / 127.0,
                                      OP.mult, OP.max)
                s_ve.inc(ins, 1)
            blk.vector(q1_v2)
            b_ve3 = s_ve.v

            def q1_sR(s, b_ve3=b_ve3):
                s.wait_ge(s_ve.h, b_ve3)
                s.activation(msc[0:1, 1:2], msc[0:1, 0:1], AF.Ln)
                ins = s.activation(msc[0:1, 1:2], msc[0:1, 1:2], AF.Exp,
                                   scale=-1.0)
                s_sc.inc(ins, 1)
            blk.scalar(q1_sR)
            sc_R = s_sc.v

            def q1_t2(t, b_ve3=b_ve3, sc_R=sc_R):
                t.wait_ge(s_ve.h, b_ve3)
                t.wait_ge(s_sc.h, sc_R)
                ins = t.matmul(p_c[:, 0:1], ones1[:, :], msc[0:1, 1:2],
                               start=True, stop=True)
                s_pe.inc(ins, 1)
            blk.tensor(q1_t2)
            b_pe2 = s_pe.v

            def q1_v3(v, mcol=mcol, b_pe2=b_pe2):
                v.wait_ge(s_pe.h, b_pe2)
                ins = v.tensor_copy(acc1[:, :], p_c[:, :])
                s_ve.inc(ins, 1)
            blk.vector(q1_v3)
            b_ve3b = s_ve.v

            def q1_t3(t, b_ve3b=b_ve3b, b_ve3=b_ve3):
                t.wait_ge(s_ve.h, b_ve3b)
                ins = t.matmul(p_c[:, 0:1], ones1[:, :], msc[0:1, 2:3],
                               start=True, stop=True)
                s_pe.inc(ins, 1)
            blk.tensor(q1_t3)
            b_pe3 = s_pe.v

            def q1_v4(v, mcol=mcol, b_pe3=b_pe3):
                v.wait_ge(s_pe.h, b_pe3)
                ins = v.tensor_copy(mcol[:, :], p_c[:, :])
                s_ve.inc(ins, 1)
            blk.vector(q1_v4)

            b_ld2 = s_ld.v
            b_ve4 = s_ve.v
            b_gs = s_gs.v
            def q2_sync(sy, wsrc=wsrc, nk=nk, ncols=ncols, b_ve4=b_ve4):
                for kt in range(nk):
                    if kt > 0:
                        sy.wait_ge(s_ve.h, b_ve4 + kt)
                    else:
                        sy.wait_ge(s_ve.h, b_ve4)
                    s_ld.inc(sy.dma_start(out=wstage[:, 0:ncols],
                                          in_=wsrc[kt * 128:(kt + 1) * 128, :]), 16)
            blk.sync(q2_sync)

            def q2_v(v, nk=nk, ncols=ncols, b_ld2=b_ld2, b_gs=b_gs):
                for kt in range(nk):
                    v.wait_ge(s_ld.h, b_ld2 + 16 * (kt + 1))
                    if kt > 0:
                        v.wait_ge(s_gs.h, b_gs + 16 * kt)
                    v.tensor_scalar(wstage[:, 0:ncols], wstage[:, 0:ncols],
                                    acc1[:, 0:1], None, OP.mult)
                    v.tensor_copy(wint[:, 0:ncols], wstage[:, 0:ncols])
                    v.tensor_copy(wstage[:, 0:ncols], wint[:, 0:ncols])
                    ins = v.tensor_scalar(wq16[:, 0:ncols], wstage[:, 0:ncols],
                                          -1.0, 1.0, OP.max, OP.min)
                    s_ve.inc(ins, 1)
            blk.vector(q2_v)
            b_ve5 = s_ve.v - nk

            def q2_g(g, wdst=wdst, nk=nk, ncols=ncols, b_ve5=b_ve5):
                for kt in range(nk):
                    g.wait_ge(s_ve.h, b_ve5 + kt + 1)
                    ins = g.dma_start(out=wdst[kt * 128:(kt + 1) * 128, :],
                                      in_=wq16[:, 0:ncols])
                    s_gs.inc(ins, 16)
            blk.gpsimd(q2_g)

    gs_p0_end = s_gs.v
    ve_p0_end = s_ve.v
    with contextlib.ExitStack() as ph:
        Eh = ph.enter_context
        winb = Eh(nc.sbuf_tensor("winb", [128, 8 * DI], BF16))
        xpb = Eh(nc.sbuf_tensor("xpb", [128, 8 * 128], BF16))
        wstg = Eh(nc.sbuf_tensor("wstg", [128, DI], F32))
        xc = Eh(nc.sbuf_tensor("xc", [128, 8 * TC], F32))
        sq = Eh(nc.sbuf_tensor("sq", [128, 2 * TC], F32))
        convt = Eh(nc.sbuf_tensor("convt", [128, TC], F32))
        rrow = Eh(nc.sbuf_tensor("rrow", [1, TC], F32))
        xn = Eh(nc.sbuf_tensor("xn", [128, 8 * TC], BF16))
        upre = Eh(nc.sbuf_tensor("upre", [128, 8 * (TC + 3)], F32))
        ubuf = Eh(nc.sbuf_tensor("ubuf", [128, 8 * TC], F32))
        ub16 = Eh(nc.sbuf_tensor("ub16", [128, 8 * TC], BF16))
        srbuf = Eh(nc.sbuf_tensor("srbuf", [128, 8 * TC], F32))
        dbcmy = Eh(nc.sbuf_tensor("dbcmy", [128, T], F32))
        peer = Eh(nc.sbuf_tensor("peer", [128, T], F32))
        p_ip = [Eh(nc.psum_tensor(f"p_ip{i}", [128, TC], F32)) for i in range(4)]
        p_ssq = Eh(nc.psum_tensor("p_ssq", [1, TC], F32))
        p_rbc = Eh(nc.psum_tensor("p_rbc", [128, TC], F32))

        b_ld = s_ld.v
        b_ve = s_ve.v
        def s1w_sync(sy, b_ve=b_ve):
            sy.wait_ge(s_gs.h, gs_p0_end)
            sy.wait_ge(s_ve.h, ve_p0_end)
            for kt in range(8):
                if kt > 0:
                    sy.wait_ge(s_ve.h, b_ve + kt)
                s_ld.inc(sy.dma_start(out=wstg[:, :],
                                      in_=w_in[kt * 128:(kt + 1) * 128, :]), 16)
            sy.wait_ge(s_ve.h, b_ve + 8)
            s_ld.inc(sy.dma_start(
                out=wstg[:, 0:8 * 128].rearrange("p (a c) -> p a c", a=8),
                in_=bass.AP(xp_sl.tensor, 0, [[128, 128], [128 * 128, 8], [1, 128]])), 16)
        blk.sync(s1w_sync)

        def s1w_v(v, b_ld=b_ld):
            for kt in range(8):
                v.wait_ge(s_ld.h, b_ld + 16 * (kt + 1))
                ins = v.tensor_copy(winb[:, kt * DI:(kt + 1) * DI], wstg[:, :])
                s_ve.inc(ins, 1)
            v.wait_ge(s_ld.h, b_ld + 16 * 9)
            ins = v.tensor_copy(xpb[:, :], wstg[:, 0:8 * 128])
            s_ve.inc(ins, 1)
        blk.vector(s1w_v)

        pve, psc, pgs = s_ve.v, s_sc.v, s_gs.v
        for c in range(NCH):
            b_ld = s_ld.v
            b_pe = s_pe.v
            b_sc = s_sc.v
            b_ve = s_ve.v

            def c_sync(sy, c=c, pve=pve, psc=psc, pgs=pgs):
                sy.wait_ge(s_ve.h, pve)
                sy.wait_ge(s_sc.h, psc)
                sy.wait_ge(s_gs.h, pgs)
                for kt in range(8):
                    s_ld.inc(sy.dma_start(
                        out=xc[:, kt * TC:(kt + 1) * TC],
                        in_=x_fm[kt * 128:(kt + 1) * 128, c * TC:(c + 1) * TC]), 16)
            blk.sync(c_sync)
            ld_x = s_ld.v

            def c_sc1(s, c=c, b_pe=b_pe, ld_x=ld_x):
                s.wait_ge(s_ld.h, ld_x)
                for kt in range(8):
                    if kt >= 2:
                        s.wait_ge(s_pe.h, b_pe + kt - 1)
                    ins = s.activation(sq[:, (kt % 2) * TC:(kt % 2 + 1) * TC],
                                       xc[:, kt * TC:(kt + 1) * TC], AF.Square)
                    s_sc.inc(ins, 1)
                s.wait_ge(s_pe.h, b_pe + 8)
                s.activation(rrow[:, :], p_ssq[:, :], AF.Ln, bias=EPS, scale=1.0 / DM)
                ins = s.activation(rrow[:, :], rrow[:, :], AF.Exp, scale=-0.5)
                s_sc.inc(ins, 1)
            blk.scalar(c_sc1)

            def c_te1(t, c=c, b_sc=b_sc, b_ve=b_ve):
                for kt in range(8):
                    t.wait_ge(s_sc.h, b_sc + kt + 1)
                    ins = t.matmul(p_ssq[:, :], onescol[:, :],
                                   sq[:, (kt % 2) * TC:(kt % 2 + 1) * TC],
                                   start=(kt == 0), stop=(kt == 7))
                    s_pe.inc(ins, 1)
                t.wait_ge(s_sc.h, b_sc + 9)
                ins = t.matmul(p_rbc[:, :], ones1[:, :], rrow[:, :],
                               start=True, stop=True)
                s_pe.inc(ins, 1)
            blk.tensor(c_te1)

            def c_ve1(v, c=c, b_pe=b_pe, ld_x=ld_x):
                v.wait_ge(s_ld.h, ld_x)
                v.wait_ge(s_pe.h, b_pe + 9)
                for kt in range(8):
                    ins = v.scalar_tensor_tensor(
                        xn[:, kt * TC:(kt + 1) * TC], xc[:, kt * TC:(kt + 1) * TC],
                        nw[:, kt:kt + 1], p_rbc[:, :], OP.mult, OP.mult)
                s_ve.inc(ins, 1)
            blk.vector(c_ve1)

            def c_te2(t, c=c, b_sc=b_sc, b_ve=b_ve, b_pe=b_pe):
                t.wait_ge(s_ve.h, b_ve + 1)
                for ot in range(16):
                    if ot >= 4:
                        if ot - 4 < 8:
                            t.wait_ge(s_ve.h, b_ve + 2 + (ot - 4))
                        else:
                            t.wait_ge(s_sc.h, b_sc + 18 + (ot - 4 - 8))
                    pb = p_ip[ot % 4]
                    for kt in range(8):
                        ins = t.matmul(pb[:, :],
                                       winb[:, kt * DI + ot * 128:kt * DI + (ot + 1) * 128],
                                       xn[:, kt * TC:(kt + 1) * TC],
                                       start=(kt == 0), stop=(kt == 7))
                    s_pe.inc(ins, 1)
                t.wait_ge(s_sc.h, b_sc + 18 + 4)
                t.wait_ge(s_ve.h, b_ve + 25)
                for kt in range(8):
                    ins = t.matmul(p_ip[0][:, :], xpb[:, kt * 128:(kt + 1) * 128],
                                   ub16[:, kt * TC:(kt + 1) * TC],
                                   start=(kt == 0), stop=(kt == 7))
                s_pe.inc(ins, 1)
            blk.tensor(c_te2)

            def c_ve2(v, c=c, b_pe=b_pe, b_sc=b_sc, b_ve=b_ve):
                for ot in range(8):
                    v.wait_ge(s_pe.h, b_pe + 10 + ot)
                    ub = ot * (TC + 3)
                    if c == 0:
                        v.memset(upre[:, ub:ub + 3], 0.0)
                    ins = v.tensor_copy(upre[:, ub + 3:ub + 3 + TC], p_ip[ot % 4][:, :])
                    s_ve.inc(ins, 1)
                for ot in range(8):
                    ub = ot * (TC + 3)
                    v.tensor_scalar(convt[:, :], upre[:, ub:ub + TC],
                                    cwm[:, ot * 4:ot * 4 + 1], None, OP.mult)
                    for k in range(1, 4):
                        v.scalar_tensor_tensor(convt[:, :], upre[:, ub + k:ub + k + TC],
                                               cwm[:, ot * 4 + k:ot * 4 + k + 1],
                                               convt[:, :], OP.mult, OP.add)
                    ins = v.tensor_copy(ubuf[:, ot * TC:(ot + 1) * TC], convt[:, :])
                    s_ve.inc(ins, 1)
                    v.tensor_copy(upre[:, ub:ub + 3], upre[:, ub + TC:ub + TC + 3])
                for ot in range(8):
                    v.wait_ge(s_sc.h, b_sc + 10 + ot)
                    ins = v.tensor_copy(ub16[:, ot * TC:(ot + 1) * TC],
                                        ubuf[:, ot * TC:(ot + 1) * TC])
                    s_ve.inc(ins, 1)
                v.wait_ge(s_pe.h, b_pe + 26)
                ins = v.tensor_copy(dbcmy[:, c * TC:(c + 1) * TC], p_ip[0][:, :])
                s_ve.inc(ins, 1)
            blk.vector(c_ve2)

            def c_sc2(s, c=c, b_ve=b_ve, b_pe=b_pe):
                for ot in range(8):
                    s.wait_ge(s_ve.h, b_ve + 10 + ot)
                    ins = s.activation(ubuf[:, ot * TC:(ot + 1) * TC],
                                       ubuf[:, ot * TC:(ot + 1) * TC],
                                       AF.Silu, bias=cbm[:, ot:ot + 1])
                    s_sc.inc(ins, 1)
                for r in range(8):
                    s.wait_ge(s_pe.h, b_pe + 10 + 8 + r)
                    ins = s.activation(srbuf[:, r * TC:(r + 1) * TC],
                                       p_ip[(8 + r) % 4][:, :], AF.Silu)
                    s_sc.inc(ins, 1)
            blk.scalar(c_sc2)

            def c_g(g, c=c, b_sc=b_sc):
                g.wait_ge(s_sc.h, b_sc + 25)
                for ot in range(8):
                    s_gs.inc(g.dma_start(
                        out=u_sp[ot * 128:(ot + 1) * 128, c * TC:(c + 1) * TC],
                        in_=ubuf[:, ot * TC:(ot + 1) * TC]), 16)
                    s_gs.inc(g.dma_start(
                        out=sr_sp[ot * 128:(ot + 1) * 128, c * TC:(c + 1) * TC],
                        in_=srbuf[:, ot * TC:(ot + 1) * TC]), 16)
            blk.gpsimd(c_g)

            assert s_sc.v == b_sc + 25
            assert s_pe.v == b_pe + 26
            assert s_ve.v == b_ve + 26
            pve, psc, pgs = s_ve.v, s_sc.v, s_gs.v

        b_ve = s_ve.v
        def ar1_g(g, b_ve=b_ve):
            g.wait_ge(s_ve.h, b_ve)
            ins = g.indirect_dma_start(
                bass.AP(sh_dbc, 0, [[T, 256], [1, T]]),
                bass.IndirectOffsetOnAxis(swidx[:, 8:9], 0),
                dbcmy[:, :], None)
            s_gs.inc(ins, 16)
            g.wait_ge(s_gs.h, s_gs.v)
            g_flag(g, 0)
            g_poll(g, 0)
            ins = g.indirect_dma_start(
                peer[:, :], None,
                bass.AP(sh_dbc, 0, [[T, 256], [1, T]]),
                bass.IndirectOffsetOnAxis(swidx[:, 9:10], 0))
            s_gs.inc(ins, 16)
            g.wait_ge(s_gs.h, s_gs.v)
            ins = g.nop(); s_gg.inc(ins, 1)
        blk.gpsimd(ar1_g)
        gg1 = s_gg.v

        def ar1_v(v, gg1=gg1):
            v.wait_ge(s_gg.h, gg1)
            v.tensor_add(dbcmy[:, :], dbcmy[:, :], peer[:, :])
            ins = v.tensor_copy(dtb16[:, :], dbcmy[0:DTR, :])
            s_ve.inc(ins, 1)
        blk.vector(ar1_v)
        ve_dbc = s_ve.v

        def ar1_g2(g, ve_dbc=ve_dbc):
            g.wait_ge(s_ve.h, ve_dbc)
            s_gs.inc(g.dma_start(out=dbc_sp.ap(), in_=dbcmy[:, :]), 16)
            g.wait_ge(s_gs.h, s_gs.v)
            ins = g.nop(); s_gg.inc(ins, 1)
        blk.gpsimd(ar1_g2)
    gg_dbc = s_gg.v

    gs_s1_end = s_gs.v
    with contextlib.ExitStack() as ph:
        Eh = ph.enter_context
        woutb = Eh(nc.sbuf_tensor("woutb", [128, 8 * DM], BF16))
        dtwb = Eh(nc.sbuf_tensor("dtwb", [DTR, DH], BF16))
        u2 = Eh(nc.sbuf_tensor("u2", [128, 8 * TC], F32))
        unA = Eh(nc.sbuf_tensor("unA", [128, 8 * TC], F32))
        unB = Eh(nc.sbuf_tensor("unB", [128, 8 * TC], F32))
        dlt = Eh(nc.sbuf_tensor("dlt", [128, 8 * TC], F32))
        dlu = Eh(nc.sbuf_tensor("dlu", [128, 8 * TC], F32))
        spt = [Eh(nc.sbuf_tensor(f"spt{i}", [128, TC], F32)) for i in range(2)]
        rlt = [Eh(nc.sbuf_tensor(f"rlt{i}", [128, TC], F32)) for i in range(2)]
        dAb = [Eh(nc.sbuf_tensor(f"dAb{i}", [128, 8 * TC], F32)) for i in range(2)]
        dBu = Eh(nc.sbuf_tensor("dBu", [128, 2 * TC], F32))
        hW = Eh(nc.sbuf_tensor("hW", [128, 8 * TC], F32))
        ybuf = Eh(nc.sbuf_tensor("ybuf", [128, 8 * TC], F32))
        yt = Eh(nc.sbuf_tensor("yt", [128, TC], F32))
        zb16 = Eh(nc.sbuf_tensor("zb16", [128, 8 * TC], BF16))
        opev = [Eh(nc.sbuf_tensor(f"opev{i}", [128, TC], F32)) for i in range(2)]
        p_dt = [Eh(nc.psum_tensor(f"p_dt{i}", [128, TC], F32)) for i in range(4)]
        p_op = [Eh(nc.psum_tensor(f"p_op{i}", [128, TC], F32)) for i in range(4)]

        b_ld = s_ld.v
        b_ve = s_ve.v
        def s2w_sync(sy, b_ve=b_ve):
            sy.wait_ge(s_gs.h, gs_s1_end)
            sy.wait_ge(s_ve.h, b_ve)
            for kt in range(8):
                if kt > 0:
                    sy.wait_ge(s_ve.h, b_ve + kt)
                s_ld.inc(sy.dma_start(out=u2[:, 0:DM],
                                      in_=wout_sl[kt * 128:(kt + 1) * 128, :]), 16)
            sy.wait_ge(s_ve.h, b_ve + 8)
            s_ld.inc(sy.dma_start(out=u2[:, 0:DH][0:DTR, :], in_=dtw_sl), 16)
        blk.sync(s2w_sync)

        def s2w_v(v, b_ld=b_ld):
            for kt in range(8):
                v.wait_ge(s_ld.h, b_ld + 16 * (kt + 1))
                ins = v.tensor_copy(woutb[:, kt * DM:(kt + 1) * DM], u2[:, 0:DM])
                s_ve.inc(ins, 1)
            v.wait_ge(s_ld.h, b_ld + 16 * 9)
            ins = v.tensor_copy(dtwb[:, :], u2[0:DTR, 0:DH])
            s_ve.inc(ins, 1)
        blk.vector(s2w_v)

        pve, psc, pgs, ppe = s_ve.v, s_sc.v, s_gs.v, s_pe.v
        pve_zmap = None
        for c in range(NCH):
            b_ld = s_ld.v
            b_pe = s_pe.v
            b_sc = s_sc.v
            b_ve = s_ve.v
            b_gs = s_gs.v
            ZV = b_ve + 320
            OV = b_ve + 336

            def c_sync1(sy, c=c, pve=pve, psc=psc):
                sy.wait_ge(s_ve.h, pve)
                sy.wait_ge(s_sc.h, psc)
                sy.wait_ge(s_gg.h, gg_dbc)
                sy.wait_ge(s_gs.h, gs_s1_end)
                for ot in range(8):
                    s_ld.inc(sy.dma_start(
                        out=u2[:, ot * TC:(ot + 1) * TC],
                        in_=u_sp[ot * 128:(ot + 1) * 128, c * TC:(c + 1) * TC]), 16)
                for si in range(8):
                    s_ld.inc(sy.dma_start(
                        out=unA[:, si * TC:(si + 1) * TC],
                        in_=bass.AP(dbc_sp, (64 + si) * T + c * TC,
                                    [[0, 128], [1, TC]])), 16)
                    s_ld.inc(sy.dma_start(
                        out=unB[:, si * TC:(si + 1) * TC],
                        in_=bass.AP(dbc_sp, (80 + si) * T + c * TC,
                                    [[0, 128], [1, TC]])), 16)
            blk.sync(c_sync1)
            ld_u = b_ld + 128
            ld_bc0 = b_ld + 128 + 256

            def c_te1(t, c=c, b_pe=b_pe, b_sc=b_sc, b_ve=b_ve,
                      psc=psc, pve=pve):
                if c == 0:
                    t.wait_ge(s_ve.h, pve)
                    t.wait_ge(s_sc.h, psc)
                for dhi in range(8):
                    if dhi >= 4:
                        t.wait_ge(s_sc.h, b_sc + 2 * (dhi - 4) + 1)
                        t.wait_ge(s_ve.h, b_ve + 2 * (dhi - 4) + 1)
                    elif c > 0:
                        t.wait_ge(s_sc.h, psc - 144 + 2 * (dhi + 4) + 1)
                        t.wait_ge(s_ve.h, pve - 344 + 2 * (dhi + 4) + 1)
                    ins = t.matmul(p_dt[dhi % 4][:, :],
                                   dtwb[:, dhi * 128:(dhi + 1) * 128],
                                   dtb16[:, c * TC:(c + 1) * TC],
                                   start=True, stop=True)
                    s_pe.inc(ins, 1)
            blk.tensor(c_te1)

            def c_sc1(s, c=c, b_pe=b_pe, b_sc=b_sc, b_ve=b_ve, pve=pve):
                for dhi in range(8):
                    s.wait_ge(s_pe.h, b_pe + dhi + 1)
                    if dhi >= 2:
                        s.wait_ge(s_ve.h, b_ve + 2 * (dhi - 2) + 1)
                    elif c > 0:
                        s.wait_ge(s_ve.h, pve - 344 + 2 * (dhi + 6) + 1)
                    sl = dhi % 2
                    ins = s.activation(spt[sl][:, :], p_dt[dhi % 4][:, :], AF.Abs,
                                       bias=dtb[:, dhi:dhi + 1])
                    s_sc.inc(ins, 1)
                    s.activation(spt[sl][:, :], spt[sl][:, :], AF.Exp, scale=-1.0)
                    ins = s.activation(spt[sl][:, :], spt[sl][:, :], AF.Ln, bias=1.0)
                    s_sc.inc(ins, 1)
            blk.scalar(c_sc1)

            def c_ve1(v, c=c, b_pe=b_pe, b_sc=b_sc, b_ve=b_ve, ld_u=ld_u):
                v.wait_ge(s_ld.h, ld_u)
                for dhi in range(8):
                    v.wait_ge(s_pe.h, b_pe + dhi + 1)
                    v.wait_ge(s_sc.h, b_sc + 2 * dhi + 2)
                    sl = dhi % 2
                    v.tensor_scalar(rlt[sl][:, :], p_dt[dhi % 4][:, :],
                                    dtb[:, dhi:dhi + 1], 0.0, OP.add, OP.max)
                    ins = v.tensor_add(dlt[:, dhi * TC:(dhi + 1) * TC],
                                       spt[sl][:, :], rlt[sl][:, :])
                    s_ve.inc(ins, 1)
                    ins = v.tensor_mul(dlu[:, dhi * TC:(dhi + 1) * TC],
                                       dlt[:, dhi * TC:(dhi + 1) * TC],
                                       u2[:, dhi * TC:(dhi + 1) * TC])
                    s_ve.inc(ins, 1)
            blk.vector(c_ve1)

            for grp in range(2):
                def c_scA(s, c=c, grp=grp, b_sc=b_sc, b_ve=b_ve, pve=pve):
                    for dhi in range(8):
                        p = grp * 8 + dhi
                        s.wait_ge(s_ve.h, b_ve + 2 * dhi + 1)
                        if p >= 2:
                            s.wait_ge(s_ve.h, b_ve + 16 + (p - 2) * 19 + 19)
                        elif c > 0:
                            s.wait_ge(s_ve.h, pve - 344 + 16 + (p + 14) * 19 + 19)
                        for si in range(8):
                            s_g = grp * 8 + si
                            ins = s.activation(
                                dAb[p % 2][:, si * TC:(si + 1) * TC],
                                dlt[:, dhi * TC:(dhi + 1) * TC], AF.Exp,
                                scale=At[:, dhi * 16 + s_g:dhi * 16 + s_g + 1])
                            s_sc.inc(ins, 1)
                blk.scalar(c_scA)

                def c_veA(v, c=c, grp=grp, b_sc=b_sc, b_ve=b_ve,
                          ld_bc0=ld_bc0, b_ld=b_ld):
                    if grp == 0:
                        v.wait_ge(s_ld.h, ld_bc0)
                    else:
                        v.wait_ge(s_ld.h, b_ld + 768 - 128)
                    for dhi in range(8):
                        p = grp * 8 + dhi
                        for si in range(8):
                            s_g = grp * 8 + si
                            v.wait_ge(s_sc.h, b_sc + 16 + p * 8 + si + 1)
                            sl2 = si % 2
                            ins = v.tensor_mul(dBu[:, sl2 * TC:(sl2 + 1) * TC],
                                               dlu[:, dhi * TC:(dhi + 1) * TC],
                                               unA[:, si * TC:(si + 1) * TC])
                            s_ve.inc(ins, 1)
                            init = (zcol[:, 0:1] if c == 0 else
                                    carry[:, dhi * 16 + s_g:dhi * 16 + s_g + 1])
                            ins = v.tensor_tensor_scan(
                                hW[:, si * TC:(si + 1) * TC],
                                dAb[p % 2][:, si * TC:(si + 1) * TC],
                                dBu[:, sl2 * TC:(sl2 + 1) * TC],
                                init, OP.mult, OP.add)
                            s_ve.inc(ins, 1)
                        ins = v.tensor_copy(
                            carry[:, dhi * 16 + grp * 8:dhi * 16 + grp * 8 + 8]
                            .rearrange("q (s o) -> q s o", o=1),
                            hW[:, :].rearrange("q (s t) -> q s t", s=8)[:, :, TC - 1:TC])
                        s_ve.inc(ins, 1)
                        ins = v.tensor_mul(dAb[p % 2][:, :], hW[:, :], unB[:, :])
                        s_ve.inc(ins, 1)
                        red_in = dAb[p % 2][:, :].rearrange("q (s t) -> q t s", s=8)
                        if grp == 0:
                            ins = v.tensor_reduce(ybuf[:, dhi * TC:(dhi + 1) * TC],
                                                  red_in, axis=AX, op=OP.add)
                        else:
                            v.tensor_reduce(yt[:, :], red_in, axis=AX, op=OP.add)
                            ins = v.tensor_add(ybuf[:, dhi * TC:(dhi + 1) * TC],
                                               ybuf[:, dhi * TC:(dhi + 1) * TC],
                                               yt[:, :])
                        s_ve.inc(ins, 1)
                blk.vector(c_veA)

                if grp == 0:
                    def c_sync2(sy, c=c, b_ve=b_ve):
                        sy.wait_ge(s_ve.h, b_ve + 16 + 8 * 19)
                        for si in range(8):
                            s_g = 8 + si
                            s_ld.inc(sy.dma_start(
                                out=unA[:, si * TC:(si + 1) * TC],
                                in_=bass.AP(dbc_sp, (64 + s_g) * T + c * TC,
                                            [[0, 128], [1, TC]])), 16)
                            s_ld.inc(sy.dma_start(
                                out=unB[:, si * TC:(si + 1) * TC],
                                in_=bass.AP(dbc_sp, (80 + s_g) * T + c * TC,
                                            [[0, 128], [1, TC]])), 16)
                    blk.sync(c_sync2)

            def c_sync3(sy, c=c, b_ve=b_ve):
                sy.wait_ge(s_ve.h, b_ve + 16 + 16 * 19)
                for ot in range(8):
                    s_ld.inc(sy.dma_start(
                        out=unA[:, ot * TC:(ot + 1) * TC],
                        in_=sr_sp[ot * 128:(ot + 1) * 128, c * TC:(c + 1) * TC]), 16)
            blk.sync(c_sync3)
            ld_sr = s_ld.v

            def c_ve2(v, c=c, b_ve=b_ve, ld_sr=ld_sr):
                v.wait_ge(s_ld.h, ld_sr)
                for dhi in range(8):
                    v.scalar_tensor_tensor(
                        yt[:, :], u2[:, dhi * TC:(dhi + 1) * TC],
                        Dv[:, dhi:dhi + 1], ybuf[:, dhi * TC:(dhi + 1) * TC],
                        OP.mult, OP.add)
                    s_ve.inc(v.nop() if False else v.tensor_mul(
                        zb16[:, dhi * TC:(dhi + 1) * TC],
                        yt[:, :], unA[:, dhi * TC:(dhi + 1) * TC]), 2)
            blk.vector(c_ve2)

            def c_te2(t, c=c, b_pe=b_pe, b_ve=b_ve, pve=pve, ZV=ZV, OV=OV):
                t.wait_ge(s_ve.h, ZV + 16)
                for mo in range(8):
                    if mo >= 4:
                        t.wait_ge(s_ve.h, OV + (mo - 4) + 1)
                    elif c > 0:
                        t.wait_ge(s_ve.h, pve - 344 + 336 + (mo + 4) + 1)
                    pb = p_op[mo % 4]
                    for kt in range(8):
                        ins = t.matmul(pb[:, :],
                                       woutb[:, kt * DM + mo * 128:kt * DM + (mo + 1) * 128],
                                       zb16[:, kt * TC:(kt + 1) * TC],
                                       start=(kt == 0), stop=(kt == 7))
                    s_pe.inc(ins, 1)
            blk.tensor(c_te2)

            def c_ve3(v, c=c, b_pe=b_pe, b_gs=b_gs, OV=OV):
                for mo in range(8):
                    v.wait_ge(s_pe.h, b_pe + 8 + mo + 1)
                    if mo >= 2:
                        v.wait_ge(s_gs.h, b_gs + 16 * (mo - 1))
                    ins = v.tensor_copy(opev[mo % 2][:, :], p_op[mo % 4][:, :])
                    s_ve.inc(ins, 1)
            blk.vector(c_ve3)

            def c_g(g, c=c, b_gs=b_gs, OV=OV):
                for mo in range(8):
                    g.wait_ge(s_ve.h, OV + mo + 1)
                    ins = g.indirect_dma_start(
                        bass.AP(sh_out, 0, [[OUTW, 2 * DM], [1, OUTW]]),
                        bass.IndirectOffsetOnAxis(swidx[:, mo:mo + 1], 0),
                        opev[mo % 2][:, :], None,
                        element_offset=c * TC + 1)
                    s_gs.inc(ins, 16)
            blk.gpsimd(c_g)

            assert s_pe.v == b_pe + 16
            assert s_sc.v == b_sc + 144
            assert s_ve.v == b_ve + 344, (s_ve.v, b_ve)
            assert s_ld.v == b_ld + 768
            assert s_gs.v == b_gs + 128
            pve, psc, pgs, ppe = s_ve.v, s_sc.v, s_gs.v, s_pe.v

        def s2_flag(g):
            g.wait_ge(s_gs.h, s_gs.v)
            g_flag(g, 1)
        blk.gpsimd(s2_flag)

    with contextlib.ExitStack() as ph:
        Eh = ph.enter_context
        atm = Eh(nc.sbuf_tensor("atm", [128, 8 * S3C], F32))
        atp = Eh(nc.sbuf_tensor("atp", [128, 8 * S3C], F32))
        xo = Eh(nc.sbuf_tensor("xo", [128, 8 * S3C], F32))
        sq3 = Eh(nc.sbuf_tensor("sq3", [128, 2 * S3C], F32))
        x2b = Eh(nc.sbuf_tensor("x2b", [128, 8 * S3C], F32))
        r3 = Eh(nc.sbuf_tensor("r3", [1, 4 * S3C], F32))
        p_sm = Eh(nc.psum_tensor("p_sm", [1, S3C], F32))
        p_sq3 = Eh(nc.psum_tensor("p_sq3", [1, S3C], F32))
        p_mb = Eh(nc.psum_tensor("p_mb", [128, S3C], F32))
        p_rb = Eh(nc.psum_tensor("p_rb", [128, S3C], F32))

        def s3_poll(g):
            g_poll(g, 1)
            ins = g.nop(); s_gg.inc(ins, 1)
        blk.gpsimd(s3_poll)
        gg_at = s_gg.v

        pve, psc, pgs = s_ve.v, s_sc.v, s_gs.v
        for c3 in range(NS3):
            b_pe = s_pe.v
            b_sc = s_sc.v
            b_ve = s_ve.v
            b_gs = s_gs.v
            def c3_g(g, c3=c3, pve=pve, psc=psc):
                g.wait_ge(s_gg.h, gg_at)
                g.wait_ge(s_ve.h, pve)
                g.wait_ge(s_sc.h, psc)
                for mo in range(8):
                    ins = g.indirect_dma_start(
                        atm[:, mo * S3C:(mo + 1) * S3C], None,
                        bass.AP(sh_out, 0, [[1, 2 * DM * OUTW], [1, 1]]),
                        bass.IndirectOffsetOnAxis(sridx[:, c3 * 8 + mo:c3 * 8 + mo + 1], 0))
                    s_gs.inc(ins, 16)
                    ins = g.indirect_dma_start(
                        atp[:, mo * S3C:(mo + 1) * S3C], None,
                        bass.AP(sh_out, 0, [[1, 2 * DM * OUTW], [1, 1]]),
                        bass.IndirectOffsetOnAxis(sridx[:, 40 + c3 * 8 + mo:40 + c3 * 8 + mo + 1], 0))
                    s_gs.inc(ins, 16)
                ins = g.nop(); s_gg.inc(ins, 1)
            blk.gpsimd(c3_g)
            b_gg = s_gg.v
            gs_gath = s_gs.v

            def c3_sync(sy, c3=c3, pve=pve):
                sy.wait_ge(s_ve.h, pve)
                for mo in range(8):
                    s_ld.inc(sy.dma_start(
                        out=xo[:, mo * S3C:(mo + 1) * S3C],
                        in_=x_own[mo * 128:(mo + 1) * 128, c3 * S3C:(c3 + 1) * S3C]), 16)
            blk.sync(c3_sync)
            ld_xo = s_ld.v

            def c3_ve1(v, c3=c3, b_gg=b_gg, ld_xo=ld_xo, gs_gath=gs_gath):
                v.wait_ge(s_gs.h, gs_gath)
                v.wait_ge(s_ld.h, ld_xo)
                for mo in range(8):
                    v.tensor_add(atm[:, mo * S3C:(mo + 1) * S3C],
                                 atm[:, mo * S3C:(mo + 1) * S3C],
                                 atp[:, mo * S3C:(mo + 1) * S3C])
                    ins = v.tensor_add(atm[:, mo * S3C:(mo + 1) * S3C],
                                       atm[:, mo * S3C:(mo + 1) * S3C],
                                       xo[:, mo * S3C:(mo + 1) * S3C])
                    s_ve.inc(ins, 1)
            blk.vector(c3_ve1)

            def c3_sc1(s, c3=c3, b_ve=b_ve, b_pe=b_pe):
                for mo in range(8):
                    s.wait_ge(s_ve.h, b_ve + mo + 1)
                    if mo >= 2:
                        s.wait_ge(s_pe.h, b_pe + mo)
                    ins = s.activation(sq3[:, (mo % 2) * S3C:(mo % 2 + 1) * S3C],
                                       atm[:, mo * S3C:(mo + 1) * S3C], AF.Square)
                    s_sc.inc(ins, 1)
            blk.scalar(c3_sc1)

            def c3_te1(t, c3=c3, b_ve=b_ve, b_sc=b_sc):
                for mo in range(8):
                    t.wait_ge(s_ve.h, b_ve + mo + 1)
                    ins = t.matmul(p_sm[:, :], onescol[:, :],
                                   atm[:, mo * S3C:(mo + 1) * S3C],
                                   start=(mo == 0), stop=(mo == 7))
                s_pe.inc(ins, 1)
                for mo in range(8):
                    t.wait_ge(s_sc.h, b_sc + mo + 1)
                    ins = t.matmul(p_sq3[:, :], onescol[:, :],
                                   sq3[:, (mo % 2) * S3C:(mo % 2 + 1) * S3C],
                                   start=(mo == 0), stop=(mo == 7))
                    s_pe.inc(ins, 1)
            blk.tensor(c3_te1)

            def c3_ve2(v, c3=c3, b_pe=b_pe, b_sc=b_sc):
                v.wait_ge(s_pe.h, b_pe + 1)
                ins = v.tensor_scalar(r3[0:1, 0:S3C], p_sm[:, :], 1.0 / DM,
                                      None, OP.mult)
                s_ve.inc(ins, 1)
                v.wait_ge(s_pe.h, b_pe + 9)
                v.tensor_scalar(r3[0:1, 2 * S3C:3 * S3C], p_sq3[:, :], 1.0 / DM,
                                None, OP.mult)
                v.tensor_mul(r3[0:1, 3 * S3C:4 * S3C], r3[0:1, 0:S3C],
                             r3[0:1, 0:S3C])
                ins = v.tensor_sub(r3[0:1, S3C:2 * S3C], r3[0:1, 2 * S3C:3 * S3C],
                                   r3[0:1, 3 * S3C:4 * S3C])
                s_ve.inc(ins, 1)
            blk.vector(c3_ve2)

            def c3_sc2(s, b_ve=b_ve):
                s.wait_ge(s_ve.h, b_ve + 10)
                s.activation(r3[0:1, S3C:2 * S3C], r3[0:1, S3C:2 * S3C], AF.Ln,
                             bias=EPS)
                ins = s.activation(r3[0:1, S3C:2 * S3C], r3[0:1, S3C:2 * S3C],
                                   AF.Exp, scale=-0.5)
                s_sc.inc(ins, 1)
            blk.scalar(c3_sc2)

            def c3_te2(t, b_sc=b_sc, b_ve=b_ve):
                t.wait_ge(s_ve.h, b_ve + 9)
                t.wait_ge(s_sc.h, b_sc + 9)
                ins = t.matmul(p_mb[:, :], ones1[:, :], r3[0:1, 0:S3C],
                               start=True, stop=True)
                s_pe.inc(ins, 1)
                ins = t.matmul(p_rb[:, :], ones1[:, :], r3[0:1, S3C:2 * S3C],
                               start=True, stop=True)
                s_pe.inc(ins, 1)
            blk.tensor(c3_te2)

            def c3_ve3(v, b_pe=b_pe):
                v.wait_ge(s_pe.h, b_pe + 11)
                for mo in range(8):
                    v.tensor_sub(x2b[:, mo * S3C:(mo + 1) * S3C],
                                 atm[:, mo * S3C:(mo + 1) * S3C], p_mb[:, :])
                    ins = v.tensor_mul(x2b[:, mo * S3C:(mo + 1) * S3C],
                                       x2b[:, mo * S3C:(mo + 1) * S3C], p_rb[:, :])
                    s_ve.inc(ins, 1)
            blk.vector(c3_ve3)

            def c3_g2(g, c3=c3, b_ve=b_ve):
                g.wait_ge(s_ve.h, b_ve + 18)
                for mo in range(8):
                    ins = g.dma_start(
                        out=x2_sp[mo * 128:(mo + 1) * 128, c3 * S3C:(c3 + 1) * S3C],
                        in_=x2b[:, mo * S3C:(mo + 1) * S3C])
                    s_gs.inc(ins, 16)
            blk.gpsimd(c3_g2)

            assert s_pe.v == b_pe + 11
            assert s_sc.v == b_sc + 9
            assert s_ve.v == b_ve + 18
            pve, psc, pgs = s_ve.v, s_sc.v, s_gs.v

    gs_s3_end = s_gs.v
    gg_x2 = None
    with contextlib.ExitStack() as ph:
        Eh = ph.enter_context
        WN = TCF + 2
        w1qb = Eh(nc.sbuf_tensor("w1qb", [128, 8 * DHID], BF16))
        x2c = Eh(nc.sbuf_tensor("x2c", [128, 8 * WN], F32))
        xq = Eh(nc.sbuf_tensor("xq", [128, 8 * WN], BF16))
        xqi = Eh(nc.sbuf_tensor("xqi", [128, WN], I32))
        qtmp = Eh(nc.sbuf_tensor("qtmp", [128, WN], F32))
        mx = Eh(nc.sbuf_tensor("mx", [128, WN], F32))
        qcol = Eh(nc.sbuf_tensor("qcol", [128, 8], F32))
        sqbc = Eh(nc.sbuf_tensor("sqbc", [128, WN], F32))
        sdqbc = Eh(nc.sbuf_tensor("sdqbc", [128, WN], F32))
        hpre = Eh(nc.sbuf_tensor("hpre", [128, 32 * WN], F32))
        hs = Eh(nc.sbuf_tensor("hs", [128, 32 * TCF], F32))
        hqb = Eh(nc.sbuf_tensor("hqb", [128, 32 * TCF], BF16))
        cvt = [Eh(nc.sbuf_tensor(f"cvt{i}", [128, TCF], F32)) for i in range(2)]
        abuf = [Eh(nc.sbuf_tensor(f"abuf{i}", [128, WN], F32)) for i in range(2)]
        sq2bc = Eh(nc.sbuf_tensor("sq2bc", [128, TCF], F32))
        p_w1 = [Eh(nc.psum_tensor(f"p_w1{i}", [128, WN], F32)) for i in range(4)]
        p_tr = Eh(nc.psum_tensor("p_tr", [128, 128], F32))

        b_ld = s_ld.v
        def s4w_sync(sy):
            sy.wait_ge(s_gs.h, gs_s3_end)
            for kt in range(8):
                s_ld.inc(sy.dma_start(out=w1qb[:, kt * DHID:(kt + 1) * DHID],
                                      in_=w1q_sp[kt * 128:(kt + 1) * 128, :]), 16)
        blk.sync(s4w_sync)
        ld_w1 = s_ld.v

        pve, psc, pgs, ppe = s_ve.v, s_sc.v, s_gs.v, s_pe.v
        for fc in range(NFF):
            b_ld = s_ld.v
            b_pe = s_pe.v
            b_sc = s_sc.v
            b_ve = s_ve.v
            b_gs = s_gs.v

            def f_sync1(sy, fc=fc, pve=pve, psc=psc, pgs=pgs):
                sy.wait_ge(s_ve.h, pve)
                sy.wait_ge(s_sc.h, psc)
                sy.wait_ge(s_gs.h, max(pgs, gs_s3_end))
                for mo in range(8):
                    s_ld.inc(sy.dma_start(
                        out=x2c[:, mo * WN:(mo + 1) * WN],
                        in_=x2_sp[mo * 128:(mo + 1) * 128, fc * TCF:fc * TCF + WN]), 16)
            blk.sync(f_sync1)
            ld_x2 = s_ld.v

            def f_sca(s, fc=fc, b_ve=b_ve, ld_x2=ld_x2):
                s.wait_ge(s_ld.h, ld_x2)
                for mo in range(8):
                    if mo >= 2:
                        s.wait_ge(s_ve.h, b_ve + max(mo - 2, 1))
                    ins = s.activation(abuf[mo % 2][:, :],
                                       x2c[:, mo * WN:(mo + 1) * WN], AF.Abs)
                    s_sc.inc(ins, 1)
            blk.scalar(f_sca)

            def f_ve1(v, fc=fc, b_sc=b_sc):
                for k in range(1, 8):
                    v.wait_ge(s_sc.h, b_sc + (2 if k == 1 else k + 1))
                    if k == 1:
                        ins = v.tensor_max(mx[:, :], abuf[0][:, :], abuf[1][:, :])
                    else:
                        ins = v.tensor_max(mx[:, :], mx[:, :], abuf[k % 2][:, :])
                    s_ve.inc(ins, 1)
            blk.vector(f_ve1)

            def f_te1b(t, fc=fc, b_ve=b_ve):
                for b in range(3):
                    bw = 128 if b < 2 else 2
                    t.wait_ge(s_ve.h, b_ve + 7 + b)
                    ins = t.transpose(p_tr[0:bw, :], mx[:, b * 128:b * 128 + bw],
                                      iden[:, :])
                    s_pe.inc(ins, 1)
            blk.tensor(f_te1b)

            def f_ve2(v, fc=fc, b_pe=b_pe, b_ve=b_ve):
                for b in range(3):
                    bw = 128 if b < 2 else 2
                    v.wait_ge(s_pe.h, b_pe + b + 1)
                    ins = v.tensor_reduce(qcol[0:bw, b:b + 1], p_tr[0:bw, :],
                                          axis=AX, op=OP.max)
                    s_ve.inc(ins, 1)
                v.wait_ge(s_ve.h, b_ve + 10)
                for b in range(3):
                    bw = 128 if b < 2 else 2
                    v.scalar_tensor_tensor(qcol[0:bw, 4 + b:5 + b],
                                           qcol[0:bw, b:b + 1], 1e-5,
                                           mcol1[0:bw, 0:1], OP.max, OP.mult)
                    ins = v.tensor_scalar(qcol[0:bw, b:b + 1],
                                          qcol[0:bw, b:b + 1],
                                          1e-5, None, OP.max)
                s_ve.inc(ins, 1)
            blk.vector(f_ve2)

            def f_scR(s, fc=fc, b_ve=b_ve):
                s.wait_ge(s_ve.h, b_ve + 11)
                for b in range(3):
                    bw = 128 if b < 2 else 2
                    s.activation(qcol[0:bw, b:b + 1], qcol[0:bw, b:b + 1], AF.Ln)
                    ins = s.activation(qcol[0:bw, b:b + 1], qcol[0:bw, b:b + 1],
                                       AF.Exp, scale=-1.0, bias=cst[0:bw, 8:9])
                s_sc.inc(ins, 1)
            blk.scalar(f_scR)

            def f_g1(g, fc=fc, b_ve=b_ve, b_sc=b_sc):
                g.wait_ge(s_ve.h, b_ve + 11)
                g.wait_ge(s_sc.h, b_sc + 9)
                for b in range(3):
                    bw = 128 if b < 2 else 2
                    ins = g.dma_start(out=bass.AP(bounce, b * 128, [[1, bw]]),
                                      in_=qcol[0:bw, b:b + 1])
                    s_gs.inc(ins, 16)
                    ins = g.dma_start(out=bass.AP(bounce, 512 + b * 128, [[1, bw]]),
                                      in_=qcol[0:bw, 4 + b:5 + b])
                    s_gs.inc(ins, 16)
            blk.gpsimd(f_g1)
            gs_b1 = s_gs.v

            def f_sync2(sy, fc=fc, gs_b1=gs_b1):
                sy.wait_ge(s_gs.h, gs_b1)
                s_ld.inc(sy.dma_start(out=sqbc[:, :],
                                      in_=bass.AP(bounce, 0, [[0, 128], [1, WN]])), 16)
                s_ld.inc(sy.dma_start(out=sdqbc[:, :],
                                      in_=bass.AP(bounce, 512, [[0, 128], [1, WN]])), 16)
            blk.sync(f_sync2)
            ld_bc = s_ld.v

            def f_ve3(v, fc=fc, ld_bc=ld_bc, ld_w1=ld_w1):
                v.wait_ge(s_ld.h, max(ld_bc, ld_w1))
                for mo in range(8):
                    v.tensor_mul(qtmp[:, :], x2c[:, mo * WN:(mo + 1) * WN],
                                 sqbc[:, :])
                    v.tensor_copy(xqi[:, :], qtmp[:, :])
                    ins = v.tensor_copy(xq[:, mo * WN:(mo + 1) * WN], xqi[:, :])
                if fc == 0:
                    for mo in range(8):
                        ins = v.tensor_scalar(xq[:, mo * WN:mo * WN + 1],
                                              xq[:, mo * WN:mo * WN + 1],
                                              maskc[:, 0:1], None, OP.mult)
                if fc == NFF - 1:
                    for mo in range(8):
                        ins = v.tensor_scalar(xq[:, mo * WN + WN - 1:mo * WN + WN],
                                              xq[:, mo * WN + WN - 1:mo * WN + WN],
                                              maskc[:, 1:2], None, OP.mult)
                s_ve.inc(ins, 1)
            blk.vector(f_ve3)

            def f_te2(t, fc=fc, b_ve=b_ve):
                t.wait_ge(s_ve.h, b_ve + 12)
                for ho in range(32):
                    if ho >= 4:
                        t.wait_ge(s_ve.h, b_ve + 13 + (ho - 4))
                    pb = p_w1[ho % 4]
                    for kt in range(8):
                        ins = t.matmul(pb[:, :],
                                       w1qb[:, kt * DHID + ho * 128:kt * DHID + (ho + 1) * 128],
                                       xq[:, kt * WN:(kt + 1) * WN],
                                       start=(kt == 0), stop=(kt == 7))
                    s_pe.inc(ins, 1)
            blk.tensor(f_te2)

            def f_ve4(v, fc=fc, b_pe=b_pe, b_sc=b_sc, b_ve=b_ve):
                for ho in range(32):
                    v.wait_ge(s_pe.h, b_pe + 3 + ho + 1)
                    ins = v.tensor_mul(hpre[:, ho * WN:(ho + 1) * WN],
                                       p_w1[ho % 4][:, :], sdqbc[:, :])
                    s_ve.inc(ins, 1)
                for ho in range(32):
                    if ho >= 2:
                        v.wait_ge(s_sc.h, b_sc + 10 + (ho - 2))
                    hb = ho * WN
                    v.tensor_scalar(cvt[ho % 2][:, :], hpre[:, hb:hb + TCF],
                                    fcw[:, ho * 3:ho * 3 + 1], None, OP.mult)
                    v.scalar_tensor_tensor(cvt[ho % 2][:, :],
                                           hpre[:, hb + 1:hb + 1 + TCF],
                                           fcw[:, ho * 3 + 1:ho * 3 + 2],
                                           cvt[ho % 2][:, :], OP.mult, OP.add)
                    ins = v.scalar_tensor_tensor(cvt[ho % 2][:, :],
                                                 hpre[:, hb + 2:hb + 2 + TCF],
                                                 fcw[:, ho * 3 + 2:ho * 3 + 3],
                                                 cvt[ho % 2][:, :], OP.mult, OP.add)
                    s_ve.inc(ins, 1)
            blk.vector(f_ve4)

            def f_sc1(s, fc=fc, b_ve=b_ve):
                for ho in range(32):
                    s.wait_ge(s_ve.h, b_ve + 45 + ho)
                    ins = s.activation(hs[:, ho * TCF:(ho + 1) * TCF],
                                       cvt[ho % 2][:, :], AF.Silu,
                                       bias=fcb[:, ho:ho + 1])
                    s_sc.inc(ins, 1)
                for ho in range(32):
                    if ho >= 2:
                        s.wait_ge(s_ve.h, b_ve + 76 + max(ho - 2, 1))
                    ins = s.activation(abuf[ho % 2][:, 0:TCF],
                                       hs[:, ho * TCF:(ho + 1) * TCF], AF.Abs)
                    s_sc.inc(ins, 1)
            blk.scalar(f_sc1)

            def f_ve5(v, fc=fc, b_sc=b_sc):
                for k in range(1, 32):
                    v.wait_ge(s_sc.h, b_sc + 42 + (1 if k == 1 else k))
                    if k == 1:
                        ins = v.tensor_max(mx[:, 0:TCF], abuf[0][:, 0:TCF],
                                           abuf[1][:, 0:TCF])
                    else:
                        ins = v.tensor_max(mx[:, 0:TCF], mx[:, 0:TCF],
                                           abuf[k % 2][:, 0:TCF])
                    s_ve.inc(ins, 1)
            blk.vector(f_ve5)

            def f_te3(t, fc=fc, b_ve=b_ve):
                for b in range(2):
                    t.wait_ge(s_ve.h, b_ve + 107 + b)
                    ins = t.transpose(p_tr[0:128, :], mx[:, b * 128:(b + 1) * 128],
                                      iden[:, :])
                    s_pe.inc(ins, 1)
            blk.tensor(f_te3)

            def f_ve6(v, fc=fc, b_pe=b_pe, b_ve=b_ve):
                for b in range(2):
                    v.wait_ge(s_pe.h, b_pe + 35 + b + 1)
                    ins = v.tensor_reduce(qcol[:, b:b + 1], p_tr[:, :],
                                          axis=AX, op=OP.max)
                    s_ve.inc(ins, 1)
                v.wait_ge(s_ve.h, b_ve + 109)
                for b in range(2):
                    v.scalar_tensor_tensor(qcol[:, 4 + b:5 + b],
                                           qcol[:, b:b + 1], 1e-5,
                                           mcol2[:, 0:1], OP.max, OP.mult)
                    ins = v.tensor_scalar(qcol[:, b:b + 1], qcol[:, b:b + 1],
                                          1e-5, None, OP.max)
                s_ve.inc(ins, 1)
            blk.vector(f_ve6)

            def f_scR2(s, fc=fc, b_ve=b_ve):
                s.wait_ge(s_ve.h, b_ve + 110)
                for b in range(2):
                    s.activation(qcol[:, b:b + 1], qcol[:, b:b + 1], AF.Ln)
                    ins = s.activation(qcol[:, b:b + 1], qcol[:, b:b + 1],
                                       AF.Exp, scale=-1.0, bias=cst[:, 8:9])
                s_sc.inc(ins, 1)
            blk.scalar(f_scR2)

            def f_g2(g, fc=fc, b_ve=b_ve, b_sc=b_sc):
                g.wait_ge(s_ve.h, b_ve + 110)
                g.wait_ge(s_sc.h, b_sc + 74)
                for b in range(2):
                    ins = g.dma_start(out=bass.AP(bounce, 1024 + b * 128, [[1, 128]]),
                                      in_=qcol[:, b:b + 1])
                    s_gs.inc(ins, 16)
                    ins = g.dma_start(out=bass.AP(bounce, 1536 + b * 128, [[1, 128]]),
                                      in_=qcol[:, 4 + b:5 + b])
                    s_gs.inc(ins, 16)
                g.wait_ge(s_gs.h, s_gs.v)
                ins = g.dma_start(out=sdq2_sp[0:1, fc * TCF:(fc + 1) * TCF],
                                  in_=bass.AP(bounce, 1536, [[1, TCF]]))
                s_gs.inc(ins, 16)
            blk.gpsimd(f_g2)
            gs_b2 = s_gs.v

            def f_sync3(sy, fc=fc, gs_b2=gs_b2):
                sy.wait_ge(s_gs.h, gs_b2)
                s_ld.inc(sy.dma_start(out=sq2bc[:, :],
                                      in_=bass.AP(bounce, 1024, [[0, 128], [1, TCF]])), 16)
            blk.sync(f_sync3)
            ld_q2 = s_ld.v

            def f_ve7(v, fc=fc, ld_q2=ld_q2):
                v.wait_ge(s_ld.h, ld_q2)
                for ho in range(32):
                    v.tensor_mul(qtmp[:, 0:TCF], hs[:, ho * TCF:(ho + 1) * TCF],
                                 sq2bc[:, :])
                    v.tensor_copy(xqi[:, 0:TCF], qtmp[:, 0:TCF])
                    ins = v.tensor_copy(hqb[:, ho * TCF:(ho + 1) * TCF],
                                        xqi[:, 0:TCF])
                    s_ve.inc(ins, 1)
            blk.vector(f_ve7)

            def f_g3(g, fc=fc, b_ve=b_ve):
                for ho in range(32):
                    g.wait_ge(s_ve.h, b_ve + 110 + ho + 1)
                    ins = g.dma_start(
                        out=hq_sp[ho * 128:(ho + 1) * 128, fc * TCF:(fc + 1) * TCF],
                        in_=hqb[:, ho * TCF:(ho + 1) * TCF])
                    s_gs.inc(ins, 16)
            blk.gpsimd(f_g3)

            assert s_ve.v == b_ve + 142
            assert s_sc.v == b_sc + 74
            assert s_pe.v == b_pe + 37
            pve, psc, pgs, ppe = s_ve.v, s_sc.v, s_gs.v, s_pe.v

    gs_s4a_end = s_gs.v
    with contextlib.ExitStack() as ph:
        Eh = ph.enter_context
        w2qb = Eh(nc.sbuf_tensor("w2qb", [128, 32 * DM], BF16))
        hq2 = Eh(nc.sbuf_tensor("hq2", [128, 32 * TCB], BF16))
        x2r = Eh(nc.sbuf_tensor("x2r", [128, 8 * TCB], F32))
        resb = Eh(nc.sbuf_tensor("resb", [128, 8 * TCB], F32))
        sq4 = Eh(nc.sbuf_tensor("sq4", [128, 2 * TCB], F32))
        outb = Eh(nc.sbuf_tensor("outb", [128, 8 * TCB], F32))
        r4 = Eh(nc.sbuf_tensor("r4", [1, 4 * TCB], F32))
        sdq2b = Eh(nc.sbuf_tensor("sdq2b", [128, TCB], F32))
        p_w2 = [Eh(nc.psum_tensor(f"p_w2{i}", [128, TCB], F32)) for i in range(4)]
        p_s4 = Eh(nc.psum_tensor("p_s4", [1, TCB], F32))
        p_q4 = Eh(nc.psum_tensor("p_q4", [1, TCB], F32))
        p_mb4 = Eh(nc.psum_tensor("p_mb4", [128, TCB], F32))
        p_rb4 = Eh(nc.psum_tensor("p_rb4", [128, TCB], F32))

        b_ld = s_ld.v
        def s4bw_sync(sy):
            sy.wait_ge(s_gs.h, gs_s4a_end)
            for kt in range(32):
                s_ld.inc(sy.dma_start(out=w2qb[:, kt * DM:(kt + 1) * DM],
                                      in_=w2q_sp[kt * 128:(kt + 1) * 128, :]), 16)
        blk.sync(s4bw_sync)
        ld_w2 = s_ld.v

        pve, psc, pgs = s_ve.v, s_sc.v, s_gs.v
        st_total = None
        for oc in range(NFB):
            b_ld = s_ld.v
            b_pe = s_pe.v
            b_sc = s_sc.v
            b_ve = s_ve.v

            def o_sync(sy, oc=oc, pve=pve, psc=psc):
                sy.wait_ge(s_ve.h, pve)
                sy.wait_ge(s_sc.h, psc)
                sy.wait_ge(s_gs.h, gs_s4a_end)
                for kt in range(32):
                    s_ld.inc(sy.dma_start(
                        out=hq2[:, kt * TCB:(kt + 1) * TCB],
                        in_=hq_sp[kt * 128:(kt + 1) * 128, oc * TCB:(oc + 1) * TCB]), 16)
                s_ld.inc(sy.dma_start(
                    out=sdq2b[:, :],
                    in_=bass.AP(sdq2_sp, oc * TCB, [[0, 128], [1, TCB]])), 16)
                for mo in range(8):
                    s_ld.inc(sy.dma_start(
                        out=x2r[:, mo * TCB:(mo + 1) * TCB],
                        in_=x2_sp[mo * 128:(mo + 1) * 128, 1 + oc * TCB:1 + (oc + 1) * TCB]), 16)
            blk.sync(o_sync)
            ld_o = s_ld.v

            def o_te1(t, oc=oc, b_ve=b_ve, ld_o=ld_o, ld_w2=ld_w2):
                t.wait_ge(s_ld.h, ld_w2)
                t.wait_ge(s_ld.h, ld_o - 128)
                for mo in range(8):
                    if mo >= 4:
                        t.wait_ge(s_ve.h, b_ve + (mo - 4) + 1)
                    pb = p_w2[mo % 4]
                    for kt in range(32):
                        ins = t.matmul(pb[:, :],
                                       w2qb[:, kt * DM + mo * 128:kt * DM + (mo + 1) * 128],
                                       hq2[:, kt * TCB:(kt + 1) * TCB],
                                       start=(kt == 0), stop=(kt == 31))
                    s_pe.inc(ins, 1)
            blk.tensor(o_te1)

            def o_ve1(v, oc=oc, b_pe=b_pe, ld_o=ld_o):
                v.wait_ge(s_ld.h, ld_o)
                for mo in range(8):
                    v.wait_ge(s_pe.h, b_pe + mo + 1)
                    v.tensor_mul(outb[:, mo * TCB:(mo + 1) * TCB],
                                 p_w2[mo % 4][:, :], sdq2b[:, :])
                    ins = v.tensor_add(resb[:, mo * TCB:(mo + 1) * TCB],
                                       outb[:, mo * TCB:(mo + 1) * TCB],
                                       x2r[:, mo * TCB:(mo + 1) * TCB])
                    s_ve.inc(ins, 1)
            blk.vector(o_ve1)

            def o_sc1(s, oc=oc, b_ve=b_ve, b_pe=b_pe):
                for mo in range(8):
                    s.wait_ge(s_ve.h, b_ve + mo + 1)
                    if mo >= 2:
                        s.wait_ge(s_pe.h, b_pe + 8 + 1 + mo - 1)
                    ins = s.activation(sq4[:, (mo % 2) * TCB:(mo % 2 + 1) * TCB],
                                       resb[:, mo * TCB:(mo + 1) * TCB], AF.Square)
                    s_sc.inc(ins, 1)
            blk.scalar(o_sc1)

            def o_te2(t, oc=oc, b_ve=b_ve, b_sc=b_sc):
                for mo in range(8):
                    t.wait_ge(s_ve.h, b_ve + mo + 1)
                    ins = t.matmul(p_s4[:, :], onescol[:, :],
                                   resb[:, mo * TCB:(mo + 1) * TCB],
                                   start=(mo == 0), stop=(mo == 7))
                s_pe.inc(ins, 1)
                for mo in range(8):
                    t.wait_ge(s_sc.h, b_sc + mo + 1)
                    ins = t.matmul(p_q4[:, :], onescol[:, :],
                                   sq4[:, (mo % 2) * TCB:(mo % 2 + 1) * TCB],
                                   start=(mo == 0), stop=(mo == 7))
                    s_pe.inc(ins, 1)
            blk.tensor(o_te2)

            def o_ve2(v, oc=oc, b_pe=b_pe):
                v.wait_ge(s_pe.h, b_pe + 9)
                ins = v.tensor_scalar(r4[0:1, 0:TCB], p_s4[:, :], 1.0 / DM,
                                      None, OP.mult)
                s_ve.inc(ins, 1)
                v.wait_ge(s_pe.h, b_pe + 17)
                v.tensor_scalar(r4[0:1, 2 * TCB:3 * TCB], p_q4[:, :], 1.0 / DM,
                                None, OP.mult)
                v.tensor_mul(r4[0:1, 3 * TCB:4 * TCB], r4[0:1, 0:TCB],
                             r4[0:1, 0:TCB])
                ins = v.tensor_sub(r4[0:1, TCB:2 * TCB], r4[0:1, 2 * TCB:3 * TCB],
                                   r4[0:1, 3 * TCB:4 * TCB])
                s_ve.inc(ins, 1)
            blk.vector(o_ve2)

            def o_sc2(s, b_ve=b_ve):
                s.wait_ge(s_ve.h, b_ve + 10)
                s.activation(r4[0:1, TCB:2 * TCB], r4[0:1, TCB:2 * TCB], AF.Ln,
                             bias=EPS)
                ins = s.activation(r4[0:1, TCB:2 * TCB], r4[0:1, TCB:2 * TCB],
                                   AF.Exp, scale=-0.5)
                s_sc.inc(ins, 1)
            blk.scalar(o_sc2)

            def o_te3(t, b_sc=b_sc, b_ve=b_ve):
                t.wait_ge(s_ve.h, b_ve + 9)
                t.wait_ge(s_sc.h, b_sc + 9)
                ins = t.matmul(p_mb4[:, :], ones1[:, :], r4[0:1, 0:TCB],
                               start=True, stop=True)
                s_pe.inc(ins, 1)
                ins = t.matmul(p_rb4[:, :], ones1[:, :], r4[0:1, TCB:2 * TCB],
                               start=True, stop=True)
                s_pe.inc(ins, 1)
            blk.tensor(o_te3)

            def o_ve3(v, b_pe=b_pe):
                v.wait_ge(s_pe.h, b_pe + 19)
                for mo in range(8):
                    v.tensor_sub(outb[:, mo * TCB:(mo + 1) * TCB],
                                 resb[:, mo * TCB:(mo + 1) * TCB], p_mb4[:, :])
                    ins = v.tensor_mul(outb[:, mo * TCB:(mo + 1) * TCB],
                                       outb[:, mo * TCB:(mo + 1) * TCB],
                                       p_rb4[:, :])
                    s_ve.inc(ins, 1)
            blk.vector(o_ve3)

            def o_g(g, oc=oc, b_ve=b_ve):
                g.wait_ge(s_ve.h, b_ve + 18)
                for mo in range(8):
                    ins = g.dma_start(
                        out=out_fm[mo * 128:(mo + 1) * 128, oc * TCB:(oc + 1) * TCB],
                        in_=outb[:, mo * TCB:(mo + 1) * TCB])
                    s_gs.inc(ins, 16)
            blk.gpsimd(o_g)

            assert s_ve.v == b_ve + 18
            pve, psc, pgs = s_ve.v, s_sc.v, s_gs.v

        def fin_g(g):
            g.wait_ge(s_gs.h, s_gs.v)
        blk.gpsimd(fin_g)

    st.close()
    return nc


_NC_CACHE = [None]


def kernel(**inputs):
    x = np.asarray(inputs["L0_attn_x"], np.float32)
    norm_w = np.asarray(inputs["L0_attn_norm_w"], np.float32)
    in_proj_w = np.asarray(inputs["L0_attn_in_proj_w"], np.float32)
    conv_w = np.asarray(inputs["L0_attn_conv_w"], np.float32)
    conv_b = np.asarray(inputs["L0_attn_conv_b"], np.float32)
    x_proj_w = np.asarray(inputs["L0_attn_x_proj_w"], np.float32)
    dt_proj_w = np.asarray(inputs["L0_attn_dt_proj_w"], np.float32)
    dt_proj_b = np.asarray(inputs["L0_attn_dt_proj_b"], np.float32)
    A_log = np.asarray(inputs["L0_attn_A_log"], np.float32)
    Dvec = np.asarray(inputs["L0_attn_D"], np.float32)
    out_proj_w = np.asarray(inputs["L0_attn_out_proj_w"], np.float32)
    w1 = np.asarray(inputs["L0_ffn_w1"], np.float32)
    fcw = np.asarray(inputs["L0_ffn_conv_w"], np.float32)
    fcb = np.asarray(inputs["L0_ffn_conv_b"], np.float32)
    w2 = np.asarray(inputs["L0_ffn_w2"], np.float32)

    if _NC_CACHE[0] is None:
        _NC_CACHE[0] = build_nc()
    nc = _NC_CACHE[0]

    magic = int(np.random.randint(1, 2**30, dtype=np.int64))
    pp = np.arange(128, dtype=np.int64)
    widx_j = []
    ridx_j = []
    for j in (0, 1):
        w = np.empty((128, 10), np.int32)
        for mo in range(8):
            w[:, mo] = j * DM + mo * 128 + pp
        w[:, 8] = j * 128 + pp
        w[:, 9] = (1 - j) * 128 + pp
        widx_j.append(w)
        r = np.empty((128, 80), np.int32)
        for c3 in range(NS3):
            for mo in range(8):
                rowm = j * DM + mo * 128 + pp
                rowp = (1 - j) * DM + mo * 128 + pp
                colbase = j * TO + c3 * S3C
                r[:, c3 * 8 + mo] = rowm * OUTW + colbase
                r[:, 40 + c3 * 8 + mo] = rowp * OUTW + colbase
        ridx_j.append(r)
    w1t = np.ascontiguousarray(w1.T)
    w2t = np.ascontiguousarray(w2.T)
    fcw_c = np.ascontiguousarray(
        fcw.reshape(DHID, 3).reshape(32, 128, 3).transpose(1, 0, 2).reshape(128, 96))
    fcb_c = np.ascontiguousarray(fcb.reshape(32, 128).T)
    nw_c = np.ascontiguousarray(norm_w.reshape(8, 128).T)
    iden = np.eye(128, dtype=np.float32)

    ims = []
    for c in range(8):
        p, j = c // 2, c % 2
        hl = slice(j * DH, (j + 1) * DH)
        x_f = np.ascontiguousarray(x[p].T)
        os_ = j * TO
        xov = np.zeros((DM, TO + 2), np.float32)
        lo, hi = os_ - 1, os_ + TO + 1
        slo, shi = max(lo, 0), min(hi, T)
        xov[:, slo - lo: slo - lo + (shi - slo)] = x_f[:, slo:shi]
        ims.append({
            "x_fm": x_f,
            "x_own": xov,
            "w_in": np.ascontiguousarray(
                np.concatenate([in_proj_w[:, j * DH:(j + 1) * DH],
                                in_proj_w[:, DI + j * DH:DI + (j + 1) * DH]], axis=1)),
            "xp_sl": np.ascontiguousarray(
                np.pad(x_proj_w[hl, :], ((0, 0), (0, 32)))),
            "dtw_sl": np.ascontiguousarray(dt_proj_w[:, hl]),
            "dtb_sl": np.ascontiguousarray(dt_proj_b[hl].reshape(8, 128).T),
            "alog_sl": np.ascontiguousarray(
                A_log[hl, :].reshape(8, 128, NS).transpose(1, 0, 2).reshape(128, 128)),
            "cw_sl": np.ascontiguousarray(
                conv_w[hl, :].reshape(8, 128, 4).transpose(1, 0, 2).reshape(128, 32)),
            "cb_sl": np.ascontiguousarray(conv_b[hl].reshape(8, 128).T),
            "d_sl": np.ascontiguousarray(Dvec[hl].reshape(8, 128).T),
            "nw_al": nw_c,
            "wout_sl": np.ascontiguousarray(out_proj_w[hl, :]),
            "w1t": w1t,
            "w2t": w2t,
            "fcw_in": fcw_c,
            "fcb_in": fcb_c,
            "iden_in": iden,
            "mask_in": np.ascontiguousarray(np.stack(
                [np.full(128, 0.0 if j == 0 else 1.0, np.float32),
                 np.full(128, 0.0 if j == 1 else 1.0, np.float32)], axis=1)),
            "meta": np.array([[j, magic, magic + 1, 0, 0, 0, 0, 0]], np.int32),
            "flagrow_in": np.tile(np.concatenate(
                [np.full(16, magic, np.int32),
                 np.full(16, magic + 1, np.int32)])[None, :], (128, 1)),
            "widx_in": widx_j[j],
            "ridx_in": ridx_j[j],
            "consts": np.tile(np.array([[EPS, 1.0 / DM, 1.0, -0.5, -1.0, 127.0,
                                         1e-5, 0.0, float(np.log(127.0))]],
                                       np.float32), (128, 1)),
        })

    res = run_bass_kernel_spmd(nc, ims, core_ids=list(range(8)))
    out = np.empty((4, T, DM), np.float32)
    for c in range(8):
        p, j = c // 2, c % 2
        out[p, j * TO:(j + 1) * TO, :] = np.asarray(res.results[c]["out_fm"]).T
    return out
